# version 13 (final)
# baseline (speedup 1.0000x reference)
import contextlib
import numpy as np
import orjson
import concourse.bass as bass
import concourse.mybir as mybir
from concourse.bass_utils import run_bass_kernel_spmd


class PBass(bass.Bass):

    def __init__(self, *a, **k):
        super().__init__(*a, **k)
        self.noldw_names = set()

    def mark_noldw(self):
        f = self.m.functions[-1]
        self.noldw_names.add(f.blocks[-1].instructions[-1].name)

    def to_json_bytes(self):
        b = super().to_json_bytes()
        if not self.noldw_names:
            return b
        j = orjson.loads(b)

        def walk(o):
            if isinstance(o, dict):
                if o.get('opcode') == 'Matmult' and o.get('name') in self.noldw_names:
                    o['ldweights'] = False
                for v in o.values():
                    walk(v)
            elif isinstance(o, list):
                for v in o:
                    walk(v)
        walk(j)
        return orjson.dumps(j)

F32 = mybir.dt.float32
BF16 = mybir.dt.bfloat16
I32 = mybir.dt.int32
AF = mybir.ActivationFunctionType
ALU = mybir.AluOpType
AX = mybir.AxisListType

D = 1024
NTOK = 4096
NCH = 17
STW = 2048
S127 = 1.0 / 127.0
ATTS = 0.125
EPS = 1e-5



def _perm_order():
    def chunk(c):
        toks = []
        for b in range(3):
            blk = 3 * c + b
            for par in (0, 1):
                for s in range(20):
                    toks.append(40 * blk + par + 2 * s)
        return toks

    order = []
    for c in range(17):
        order += chunk(c)
    order += [4080 + 2 * i for i in range(8)]
    for c in range(17, 34):
        order += chunk(c)
    order += [4081 + 2 * i for i in range(8)]
    assert len(order) == 4096 and len(set(order)) == 4096
    return np.asarray(order, dtype=np.int64)


def _weight_images(Wqkv, Wout, wg, wu, wd, bout):
    WqkvT = np.ascontiguousarray(np.asarray(Wqkv).T)
    WoutT = np.ascontiguousarray(np.asarray(Wout).T)
    WgT = np.ascontiguousarray(np.asarray(wg).T)
    WuT = np.ascontiguousarray(np.asarray(wu).T)
    WdT = np.ascontiguousarray(np.asarray(wd).T)
    wqk = np.ascontiguousarray(
        WqkvT[:, 0:2048].reshape(8, 128, 16, 128).transpose(2, 1, 0, 3).reshape(2048, 1024))
    wv = np.ascontiguousarray(
        WqkvT[:, 2048:3072].reshape(8, 128, 1024).transpose(1, 0, 2).reshape(128, 8192))
    wo = np.ascontiguousarray(
        WoutT.reshape(8, 128, 1024).transpose(1, 0, 2).reshape(128, 8192))

    def gu(WT):
        return np.ascontiguousarray(
            WT.reshape(8, 128, 24, 128).transpose(2, 1, 0, 3).reshape(3072, 1024))

    wgi, wui = gu(WgT), gu(WuT)
    wdi = np.ascontiguousarray(
        WdT.reshape(2, 12, 128, 1024).transpose(0, 2, 1, 3).reshape(256, 12288))
    boutb = np.ascontiguousarray(
        np.broadcast_to(np.asarray(bout, dtype=np.float32), (128, 1024)))
    ident = np.eye(128, dtype=np.float32)
    amask = np.zeros((120, 240), dtype=np.float32)
    for gi in range(6):
        amask[gi * 20:(gi + 1) * 20, gi * 20:(gi + 1) * 20] = 1.0
    amask[:, 120:240] = amask[:, 0:120]
    return dict(wqk=wqk, wv=wv, wo=wo, wg=wgi, wu=wui, wd=wdi,
                boutb=boutb, ident=ident, amask=amask)



class Sem:
    def __init__(self, ctx, nc, name, step):
        self.h = ctx.enter_context(nc.semaphore(name))
        self.step = step
        self.n = 0

    def inc(self, instr):
        instr.then_inc(self.h, self.step)
        self.n += self.step
        return self.n


DEBUG = False
DEBUG_ST = 0


def build_nc():
    ctx = contextlib.ExitStack()
    nc = PBass("TRN2", target_bir_lowering=False)
    if DEBUG:
        dbg_xn = nc.dram_tensor("dbg_xn", [128, 16384], BF16, kind="ExternalOutput").ap()
        dbg_g = nc.dram_tensor("dbg_g", [128, 17408], BF16, kind="ExternalOutput").ap()
        dbg_v = nc.dram_tensor("dbg_v", [128, 17680], BF16, kind="ExternalOutput").ap()
        dbg_atm = nc.dram_tensor("dbg_atm", [128, 17408], BF16, kind="ExternalOutput").ap()
        dbg_r = nc.dram_tensor("dbg_r", [128, 272], F32, kind="ExternalOutput").ap()
        dbg_s = nc.dram_tensor("dbg_s", [2048, 1024], F32, kind="ExternalOutput").ap()
        dbg_qk = nc.dram_tensor("dbg_qk", [128, 4096], BF16, kind="ExternalOutput").ap()
        dbg_sc = nc.dram_tensor("dbg_sc", [128, 240], BF16, kind="ExternalOutput").ap()
        dbg_av = nc.dram_tensor("dbg_av", [128, 130], F32, kind="ExternalOutput").ap()
        dbg_av_sb = ctx.enter_context(nc.sbuf_tensor("dbg_av_sb", [128, 130], F32))

    xin = nc.dram_tensor("x_perm", [NTOK, D], F32, kind="ExternalInput").ap()
    wqk_d = nc.dram_tensor("wqk", [2048, 1024], F32, kind="ExternalInput").ap()
    wv_d = nc.dram_tensor("wv", [128, 8192], F32, kind="ExternalInput").ap()
    wo_d = nc.dram_tensor("wo", [128, 8192], F32, kind="ExternalInput").ap()
    wg_d = nc.dram_tensor("wg", [3072, 1024], I32, kind="ExternalInput").ap()
    wu_d = nc.dram_tensor("wu", [3072, 1024], I32, kind="ExternalInput").ap()
    wd_d = nc.dram_tensor("wd", [256, 12288], I32, kind="ExternalInput").ap()
    boutb_d = nc.dram_tensor("boutb", [128, 1024], F32, kind="ExternalInput").ap()
    ident_d = nc.dram_tensor("ident", [128, 128], F32, kind="ExternalInput").ap()
    amask_d = nc.dram_tensor("amask", [120, 240], F32, kind="ExternalInput").ap()
    out_d = nc.dram_tensor("out_perm", [NTOK, D], F32, kind="ExternalOutput").ap()

    xn_fm = ctx.enter_context(nc.sbuf_tensor("xn_fm", [128, 8 * STW], BF16))
    g_buf = ctx.enter_context(nc.sbuf_tensor("g_buf", [128, NCH * 1024], BF16))
    g_r0 = ctx.enter_context(nc.sbuf_tensor("g_r0", [8, 1024], BF16))
    g_r1 = ctx.enter_context(nc.sbuf_tensor("g_r1", [8, 1024], BF16))
    v_r40 = ctx.enter_context(nc.sbuf_tensor("v_r40", [40, 1040], BF16))
    v_rtmp = ctx.enter_context(nc.sbuf_tensor("v_rtmp", [8, 1040], BF16))
    q_runt = ctx.enter_context(nc.sbuf_tensor("q_runt", [128, 320], BF16))
    k_runt = ctx.enter_context(nc.sbuf_tensor("k_runt", [128, 320], BF16))
    stat = ctx.enter_context(nc.sbuf_tensor("stat", [128, 16], F32))
    eps_t = ctx.enter_context(nc.sbuf_tensor("eps_t", [128, 1], F32))
    expS = ctx.enter_context(nc.sbuf_tensor("expS", [128, 480], BF16))
    r_all = ctx.enter_context(nc.sbuf_tensor("r_all", [128, NCH * 16], F32))
    r_rec = ctx.enter_context(nc.sbuf_tensor("r_rec", [128, 32], F32))
    s_tm = ctx.enter_context(nc.sbuf_tensor("s_tm", [128, 2048], F32))
    boutb = ctx.enter_context(nc.sbuf_tensor("boutb_sb", [128, 1024], F32))
    identt = ctx.enter_context(nc.sbuf_tensor("identt", [128, 128], BF16))
    amask_sb = ctx.enter_context(nc.sbuf_tensor("amask_sb", [120, 240], BF16))
    arena = ctx.enter_context(nc.sbuf_tensor("arena", [128, 98304], mybir.dt.int8))
    ar = arena.ap()

    def abf(off, cols, rows=128):
        return ar[0:rows, off:off + 2 * cols].bitcast(BF16)

    def af32(off, cols, rows=128):
        return ar[0:rows, off:off + 4 * cols].bitcast(F32)

    A_XTM = [0, 4096]
    A_XNTM = [8192, 10240]
    A_SQ = 12288
    A_H = [k * 4096 for k in range(12)]
    A_R = [49152, 73728]
    A_WV = 73728
    A_VAUG = 16384
    A_PAIR = [0, 4096, 8192, 12288]
    A_WTILE = [86560, 88608]
    A_ATM = 51744
    A_XR = [16384, 20480]
    A_NRM = [24576, 26624]
    A_FM = [28672, 30592]
    A_WO = 32512
    A_RATM = 0
    A_RR = 2048
    A_RG = 4160
    A_RXR = 8256

    psA = ctx.enter_context(nc.psum_tensor("psA", [128, 2048], F32))
    psB = ctx.enter_context(nc.psum_tensor("psB", [128, 2048], F32))
    psAa, psBa = psA.ap(), psB.ap()

    def psA_bf(bank, cols, rows=128):
        return psAa[0:rows, bank * 512:bank * 512 + (cols + 1) // 2].bitcast(BF16)

    xl = [Sem(ctx, nc, "xlA", 16), Sem(ctx, nc, "xlB", 16)]
    sts = [Sem(ctx, nc, "stA", 16), Sem(ctx, nc, "stB", 16)]
    stm = Sem(ctx, nc, "stm", 16)
    wr = [Sem(ctx, nc, "wr0", 16)]
    wts = [Sem(ctx, nc, "wt0", 16), Sem(ctx, nc, "wt1", 16),
           Sem(ctx, nc, "wt2", 16)]
    wvo = Sem(ctx, nc, "wvo", 16)
    cst = Sem(ctx, nc, "cst", 16)
    pe = Sem(ctx, nc, "pe", 1)
    ve = Sem(ctx, nc, "ve", 1)
    se = Sem(ctx, nc, "se", 1)
    ge = Sem(ctx, nc, "ge", 1)

    def _w(eng, sem, tok):
        if tok <= 0:
            return
        assert tok <= sem.n, "wait on future token"
        eng.wait_ge(sem.h, tok)

    def wd(eng, tok):
        if tok is None:
            return
        sem, val = tok
        _w(eng, sem, val)

    T, V, S, SY, G = nc.tensor, nc.vector, nc.scalar, nc.sync, nc.gpsimd

    cst.inc(G.dma_start(identt[:, :], ident_d))
    cst.inc(G.dma_start(boutb[:, :], boutb_d))
    cst.inc(G.dma_start(amask_sb[:, :], amask_d))
    CT = (cst, cst.n)
    t_vz = ve.inc(V.memset(v_r40[0:40, :], 0.0))
    _w(V, ve, t_vz)
    t_vr_ones = ve.inc(V.memset(v_r40[0:40, 64:1040:65], 1.0))
    V.memset(q_runt[:, :], 0.0)
    t_qkr_z = ve.inc(V.memset(k_runt[:, :], 0.0))
    t_eps = ve.inc(V.memset(eps_t[:, :], EPS))

    RING_OFF = [92160, 94208, 96256]
    ring_plan = []
    for st_ in range(2):
        for hf_ in range(2):
            for ml_ in range(12):
                ring_plan.append(("u", hf_ * 12 + ml_))
            for ml_ in range(12):
                ring_plan.append(("g", hf_ * 12 + ml_))
        for p_ in range(8):
            ring_plan.append(("qk", p_))
            ring_plan.append(("qk", 8 + p_))
    ring_src = {"u": wu_d, "g": wg_d, "qk": wqk_d}
    ring_toks = []
    ring_free_pe = [0, 0, 0]
    ring_ci = [0]

    def ring_issue():
        i = len(ring_toks)
        if i >= len(ring_plan):
            return
        kind, row = ring_plan[i]
        sl = i % 3
        _w(G, pe, ring_free_pe[sl])
        ring_toks.append((wts[sl], wts[sl].inc(G.dma_start(
            abf(RING_OFF[sl], 1024), ring_src[kind][row * 128:(row + 1) * 128, :]))))

    def ring_take():
        i = ring_ci[0]
        ring_ci[0] += 1
        assert i < len(ring_toks), "ring item not issued yet"
        return i, ring_toks[i], abf(RING_OFF[i % 3], 1024)

    def ring_done(i, pe_tok):
        ring_free_pe[i % 3] = pe_tok
        ring_issue()

    for _ in range(3):
        ring_issue()

    A_WD = 49152

    def dma_wd(hf_):
        last = 0
        for kl in range(12):
            last = wr[0].inc(G.dma_start(
                abf(A_WD + kl * 2048, 1024),
                wd_d[hf_ * 128:hf_ * 128 + 128, kl * 1024:(kl + 1) * 1024]))
        return (wr[0], last)

    t_wd_tok = dma_wd(0)

    def dma_wv():
        last = 0
        for _q in range(4):
            last = wvo.inc(G.dma_start(abf(A_WV + _q * 4096, 2048),
                                       wv_d[:, _q * 2048:(_q + 1) * 2048]))
        return (wvo, last)

    def dma_wo():
        last = 0
        for _q in range(4):
            last = wvo.inc(G.dma_start(abf(A_WO + _q * 4096, 2048),
                                       wo_d[:, _q * 2048:(_q + 1) * 2048]))
        return (wvo, last)

    t_wv_tok = dma_wv()

    psA_free = [0, 0]
    psB_free = [0, 0]
    tr_free = [0, 0]
    sc_free = [0, 0]
    av_free = [0, 0]

    t_g_ready = [0, 0]
    t_runt_v = []
    t_runt_qk = []
    ident_rdy_waited = [False, False, False]

    for st in range(2):
        base = st * STW

        xld_toks = []
        t0 = (xl[0], xl[0].inc(SY.dma_start(af32(A_XTM[0], 1024, 120),
                                            xin[base:base + 120, :])))
        xld_toks.append(t0)
        apply_tok = [0, 0]
        sq_tok = [0, 0]
        var_tok = [0, 0]
        rstd_tok = [0, 0]
        tr_pe = [0, 0]
        for ci in range(NCH + 1):
            w = 120 if ci < NCH else 8
            pg = ci % 2
            if ci < NCH:
                nw = 120 if ci + 1 < NCH else 8
                off = base + (ci + 1) * 120 if ci + 1 < NCH else base + 2040
                npg = (ci + 1) % 2
                _w(SY, ve, apply_tok[npg])
                _w(SY, se, sq_tok[npg])
                xld_toks.append((xl[npg], xl[npg].inc(SY.dma_start(
                    af32(A_XTM[npg], 1024, nw), xin[off:off + nw, :]))))
            xt = af32(A_XTM[pg], 1024, w)
            sc = 8 * pg
            wd(V, xld_toks[ci])
            _w(V, ve, var_tok[pg])
            t_sum = ve.inc(V.tensor_reduce(stat[0:w, sc:sc + 1], xt, AX.X, ALU.add))
            wd(S, xld_toks[ci])
            _w(S, ve, var_tok[pg])
            t_sq = se.inc(S.activation(abf(A_SQ + pg * 2048, 1024, w), xt, AF.Square,
                                       accum_out=stat[0:w, sc + 1:sc + 2]))
            sq_tok[pg] = t_sq
            _w(V, ve, t_sum)
            t_m = ve.inc(V.tensor_scalar_mul(stat[0:w, sc + 2:sc + 3],
                                             stat[0:w, sc:sc + 1], 1.0 / D))
            _w(V, ve, t_m)
            t_msq = ve.inc(V.tensor_mul(stat[0:w, sc + 3:sc + 4],
                                        stat[0:w, sc + 2:sc + 3], stat[0:w, sc + 2:sc + 3]))
            _w(V, se, t_sq)
            _w(V, ve, t_msq)
            t_var = ve.inc(V.scalar_tensor_tensor(
                stat[0:w, sc + 4:sc + 5], stat[0:w, sc + 1:sc + 2], 1.0 / D,
                stat[0:w, sc + 3:sc + 4], ALU.mult, ALU.subtract))
            var_tok[pg] = t_var
            _w(S, ve, t_var)
            _w(S, ve, t_eps)
            _w(S, se, rstd_tok[pg])
            t_ln = se.inc(S.activation(stat[0:w, sc + 5:sc + 6],
                                       stat[0:w, sc + 4:sc + 5], AF.Ln,
                                       bias=eps_t[0:w, 0:1]))
            _w(S, se, t_ln)
            t_rstd = se.inc(S.activation(stat[0:w, sc + 6:sc + 7],
                                         stat[0:w, sc + 5:sc + 6], AF.Exp, scale=-0.5))
            rstd_tok[pg] = t_rstd
            xnt = abf(A_XNTM[pg], 1024, w)
            _w(V, se, t_rstd)
            _w(V, pe, tr_pe[pg])
            t_apply = ve.inc(V.tensor_scalar(xnt, xt, stat[0:w, sc + 2:sc + 3],
                                             stat[0:w, sc + 6:sc + 7],
                                             ALU.subtract, ALU.mult))
            apply_tok[pg] = t_apply
            tb = pg
            _w(T, ve, t_apply)
            _w(T, ve, tr_free[tb])
            if not ident_rdy_waited[0]:
                wd(T, CT)
                ident_rdy_waited[0] = True
            for i in range(8):
                mm = T.matmul(psA_bf(tb, 8 * w)[:, i * w:(i + 1) * w],
                              xnt[:, i * 128:(i + 1) * 128], identt[0:w, 0:w],
                              is_transpose=True, start=(i == 0), stop=(i == 7),
                              skip_group_check=True)
            t_tr = pe.inc(mm)
            tr_pe[tb] = t_tr
            colb = ci * 120 if ci < NCH else 2040
            dst = xn_fm[:, 0:8 * STW].rearrange("p (k c) -> p k c", k=8, c=STW)[:, :, colb:colb + w]
            src = psA_bf(tb, 8 * w).rearrange("p (k c) -> p k c", k=8, c=w)
            _w(V, pe, t_tr)
            tr_free[tb] = ve.inc(V.tensor_copy(dst, src))
        ve_xn = max(tr_free)
        if DEBUG and st == DEBUG_ST:
            _w(SY, ve, ve_xn)
            stm.inc(SY.dma_start(dbg_xn, xn_fm[:, :]))

        if st == 1:
            _w(G, ve, ve.n)
            t_wd_tok = dma_wd(0)
            t_wv_tok = dma_wv()
        for _sem in (sts[0], sts[1], stm):
            _w(S, _sem, _sem.n)
            _w(V, _sem, _sem.n)
        gtmp_free = [0, 0]
        gpart_tok = {}
        t_down_last = 0
        fill_free = [max(psA_free), max(psB_free)]
        sg_free = [0, 0]
        for hf in range(2):
            h_tok = {}
            fill_free[0] = max(fill_free[0], max(psA_free))
            fill_free[1] = max(fill_free[1], max(psB_free))
            for ml in range(12):
                ri, rtok, rview = ring_take()
                pp = ml % 2
                ps = psAa if pp == 0 else psBa
                wd(T, rtok)
                _w(T, ve, fill_free[pp])
                if ml < 2:
                    _w(T, ve, ve_xn)
                for k in range(8):
                    for j in range(4):
                        mm = T.matmul(
                            ps[:, j * 512:(j + 1) * 512],
                            rview[:, k * 128:(k + 1) * 128],
                            xn_fm[:, k * STW + j * 512:k * STW + (j + 1) * 512],
                            start=(k == 0), stop=(k == 7))
                        if j > 0:
                            nc.mark_noldw()
                t_mm = pe.inc(mm)
                _w(V, pe, t_mm)
                fill_free[pp] = ve.inc(V.tensor_copy(abf(A_H[ml], 2048),
                                                     ps[:, 0:2048]))
                h_tok[ml] = fill_free[pp]
                ring_done(ri, t_mm)

            for ml in range(12):
                ri, rtok, rview = ring_take()
                pp = ml % 2
                ps = psAa if pp == 0 else psBa
                wd(T, rtok)
                _w(T, ve, fill_free[pp])
                for k in range(8):
                    for j in range(4):
                        mm = T.matmul(
                            ps[:, j * 512:(j + 1) * 512],
                            rview[:, k * 128:(k + 1) * 128],
                            xn_fm[:, k * STW + j * 512:k * STW + (j + 1) * 512],
                            start=(k == 0), stop=(k == 7))
                        if j > 0:
                            nc.mark_noldw()
                t_mm = pe.inc(mm)
                sgp = s_tm.ap()[:, pp * 1024:(pp + 1) * 1024].bitcast(BF16)
                _w(S, pe, t_mm)
                _w(S, ve, sg_free[pp])
                t_silu = se.inc(S.activation(sgp, ps[:, 0:2048], AF.Silu, scale=S127))
                hview = abf(A_H[ml], 2048)
                _w(V, se, t_silu)
                _w(V, ve, h_tok[ml])
                tt = ve.inc(V.tensor_mul(hview, hview, sgp))
                fill_free[pp] = tt
                sg_free[pp] = tt
                ring_done(ri, t_mm)
            psA_free = [fill_free[0], fill_free[0]]
            psB_free = [fill_free[1], fill_free[1]]

            t_wd = t_wd_tok
            Rd = abf(A_WD, 12288)
            h_ready = max(psA_free)
            for ci in range(NCH + 1):
                w = 120 if ci < NCH else 8
                colb = ci * 120 if ci < NCH else 2040
                pp = ci % 2
                dstp = psBa[0:w, pp * 1024:pp * 1024 + 1024]
                wd(T, t_wd)
                _w(T, ve, psB_free[pp])
                if ci == 0:
                    _w(T, ve, h_ready)
                    if st == 0 and hf == 0:
                        _w(T, se, max(sc_free))
                for kl in range(12):
                    for j in range(2):
                        mm = T.matmul(
                            dstp[:, j * 512:(j + 1) * 512],
                            abf(A_H[kl], 2048)[:, colb:colb + w],
                            Rd[:, kl * 1024 + j * 512:kl * 1024 + (j + 1) * 512],
                            start=(kl == 0), stop=(kl == 11))
                        if j == 1:
                            nc.mark_noldw()
                t_mm = pe.inc(mm)
                t_down_last = t_mm
                if ci < NCH:
                    gdst = g_buf[0:w, ci * 1024:(ci + 1) * 1024]
                else:
                    gdst = (g_r0 if st == 0 else g_r1)[0:w, :]
                if hf == 0:
                    _w(V, pe, t_mm)
                    psB_free[pp] = ve.inc(V.tensor_scalar_mul(gdst, dstp, S127 * S127))
                    gpart_tok[ci] = psB_free[pp]
                else:
                    gtmp = s_tm.ap()[0:w, pp * 1024:pp * 1024 + 1024]
                    _w(V, pe, t_mm)
                    _w(V, se, gtmp_free[pp])
                    _w(V, ve, gpart_tok[ci])
                    t_gt = ve.inc(V.scalar_tensor_tensor(
                        gtmp, dstp, S127 * S127, gdst, ALU.mult, ALU.add))
                    psB_free[pp] = t_gt
                    _w(S, ve, t_gt)
                    t_g = se.inc(S.activation(gdst, gtmp, AF.Silu))
                    gtmp_free[pp] = t_g
            if hf == 0:
                _w(G, pe, t_down_last)
                t_wd_tok = dma_wd(1)
            t_g_ready[st] = se.n
        if DEBUG and st == DEBUG_ST:
            _w(SY, se, t_g_ready[st])
            stm.inc(SY.dma_start(dbg_g, g_buf[:, :]))

        t_wv = t_wv_tok
        vaug = abf(A_VAUG, NCH * 1040)
        _w(V, ve, ve.n)
        _w(V, pe, pe.n)
        t_ones = ve.inc(V.memset(
            vaug[0:120, :].rearrange("p (c h g) -> p c h g",
                                     c=NCH, h=16, g=65)[:, :, :, 64:65], 1.0))
        for ci in range(NCH + 1):
            w = 120 if ci < NCH else 8
            colb = ci * 120 if ci < NCH else 2040
            pp = ci % 2
            dstp = psBa[0:w, pp * 1024:pp * 1024 + 1024]
            wd(T, t_wv)
            _w(T, ve, psB_free[pp])
            for k in range(8):
                for j in range(2):
                    mm = T.matmul(
                        dstp[:, j * 512:(j + 1) * 512],
                        xn_fm[:, k * STW + colb:k * STW + colb + w],
                        abf(A_WV, 8192)[:, k * 1024 + j * 512:k * 1024 + (j + 1) * 512],
                        start=(k == 0), stop=(k == 7))
                    if j == 1:
                        nc.mark_noldw()
            t_mm = pe.inc(mm)
            _w(V, pe, t_mm)
            if ci < NCH:
                dst = vaug[0:120, ci * 1040:(ci + 1) * 1040].rearrange(
                    "p (h g) -> p h g", h=16, g=65)[:, :, 0:64]
                _w(V, ve, t_ones)
                psB_free[pp] = ve.inc(V.tensor_copy(
                    dst, dstp.rearrange("p (h g) -> p h g", h=16, g=64)))
            elif st == 0:
                dst = v_r40[0:8, :].rearrange("p (h g) -> p h g", h=16, g=65)[:, :, 0:64]
                _w(V, ve, t_vr_ones)
                psB_free[pp] = ve.inc(V.tensor_copy(
                    dst, dstp.rearrange("p (h g) -> p h g", h=16, g=64)))
            else:
                tv1 = ve.inc(V.memset(v_rtmp[0:8, :], 0.0))
                _w(V, ve, tv1)
                tv2 = ve.inc(V.memset(v_rtmp[0:8, 64:1040:65], 1.0))
                dst = v_rtmp[0:8, :].rearrange("p (h g) -> p h g", h=16, g=65)[:, :, 0:64]
                _w(V, ve, tv2)
                tv3 = ve.inc(V.tensor_copy(
                    dst, dstp.rearrange("p (h g) -> p h g", h=16, g=64)))
                psB_free[pp] = tv3
                _w(SY, ve, tv3)
                _w(SY, ve, t_vr_ones)
                t_runt_v.append((stm, stm.inc(SY.dma_start(v_r40[20:28, :],
                                                           v_rtmp[0:8, :]))))
        t_vaug = max(psB_free)
        if DEBUG and st == DEBUG_ST:
            _w(SY, ve, t_vaug)
            stm.inc(SY.dma_start(dbg_v, abf(A_VAUG, NCH * 1040)))

        pair_free = [0, 0]
        atm = abf(A_ATM, NCH * 1024)
        for p in range(8):
            pb = p % 2
            for iqk in (0, 1):
                ri, rtok, rview = ring_take()
                tile = abf(A_PAIR[2 * pb + iqk], 2048)
                t_mm = 0
                for ch in range(2):
                    pp = ch
                    dstp = psAa[:, pp * 1024:(pp + 1) * 1024]
                    wd(T, rtok)
                    _w(T, ve, psA_free[pp])
                    if p >= 2 and ch == 0 and iqk == 0:
                        _w(T, ve, pair_free[pb])
                    for k in range(8):
                        for j in range(2):
                            mm = T.matmul(
                                dstp[:, j * 512:(j + 1) * 512],
                                rview[:, k * 128:(k + 1) * 128],
                                xn_fm[:, k * STW + ch * 1024 + j * 512:
                                      k * STW + ch * 1024 + (j + 1) * 512],
                                start=(k == 0), stop=(k == 7))
                            if j == 1:
                                nc.mark_noldw()
                    t_mm = pe.inc(mm)
                    _w(V, pe, t_mm)
                    psA_free[pp] = ve.inc(V.tensor_copy(
                        tile[:, ch * 1024:(ch + 1) * 1024], dstp))
                ring_done(ri, t_mm)
            qk_rdy = max(psA_free)
            qtile = abf(A_PAIR[2 * pb + 0], 2048)
            ktile = abf(A_PAIR[2 * pb + 1], 2048)

            rc = 0 if st == 0 else 20
            _w(V, ve, t_qkr_z)
            _w(V, ve, qk_rdy)
            V.tensor_copy(q_runt[:, p * 40 + rc:p * 40 + rc + 8], qtile[:, 2040:2048])
            t_rk = ve.inc(V.tensor_copy(k_runt[:, p * 40 + rc:p * 40 + rc + 8],
                                        ktile[:, 2040:2048]))
            t_runt_qk.append(t_rk)

            exp_tok = [0] * NCH
            av_tok = [0] * NCH

            def emit_scores(c):
                cb = c * 120
                _w(T, ve, qk_rdy)
                _w(T, se, sc_free[0])
                _w(T, se, sc_free[1])
                T.matmul(psBa[0:120, 0:120], ktile[0:64, cb:cb + 120],
                         qtile[0:64, cb:cb + 120], start=True, stop=True)
                mm = T.matmul(psBa[0:120, 512:632], ktile[64:128, cb:cb + 120],
                              qtile[64:128, cb:cb + 120], start=True, stop=True)
                return pe.inc(mm)

            def emit_exp(c, t_sc):
                pg = c % 2
                _w(S, pe, t_sc)
                if c >= 2:
                    _w(S, pe, av_tok[c - 2])
                S.activation(expS[0:120, pg * 240:pg * 240 + 120],
                             psBa[0:120, 0:120], AF.Exp, scale=ATTS)
                t1 = se.inc(S.activation(expS[0:120, pg * 240 + 120:pg * 240 + 240],
                                         psBa[0:120, 512:632], AF.Exp, scale=ATTS))
                sc_free[0] = t1
                sc_free[1] = t1
                _w(V, se, t1)
                wd(V, CT)
                t2 = ve.inc(V.tensor_mul(expS[0:120, pg * 240:pg * 240 + 240],
                                         expS[0:120, pg * 240:pg * 240 + 240],
                                         amask_sb[0:120, :]))
                return t2

            def emit_attnv(c, t_msk):
                pg = c % 2
                if p == 0 and c == 0:
                    _w(T, ve, t_vaug)
                _w(T, ve, t_msk)
                _w(T, ve, av_free[0])
                _w(T, ve, av_free[1])
                T.matmul(psBa[0:120, 1024:1089], expS[0:120, pg * 240:pg * 240 + 120],
                         vaug[0:120, c * 1040 + 2 * p * 65:c * 1040 + 2 * p * 65 + 65],
                         start=True, stop=True)
                mm = T.matmul(psBa[0:120, 1536:1601],
                              expS[0:120, pg * 240 + 120:pg * 240 + 240],
                              vaug[0:120, c * 1040 + (2 * p + 1) * 65:
                                   c * 1040 + (2 * p + 1) * 65 + 65],
                              start=True, stop=True)
                return pe.inc(mm)

            def emit_evac(c, t_av):
                src = psBa[0:120, 1024:2048].rearrange("p (h x) -> p h x", h=2, x=512)
                _w(V, pe, t_av)
                V.tensor_copy(
                    atm[0:120, c * 1024 + 2 * p * 64:c * 1024 + 2 * p * 64 + 128]
                    .rearrange("p (h x) -> p h x", h=2, x=64),
                    src[:, :, 0:64])
                t2 = ve.inc(V.tensor_copy(
                    r_all[0:120, c * 16 + 2 * p:c * 16 + 2 * p + 2]
                    .rearrange("p (h x) -> p h x", h=2, x=1),
                    src[:, :, 64:65]))
                av_free[0] = t2
                av_free[1] = t2
                return t2

            if DEBUG and st == DEBUG_ST and p == 0:
                _w(SY, ve, qk_rdy)
                stm.inc(SY.dma_start(dbg_qk[:, 0:2048], qtile))
                stm.inc(SY.dma_start(dbg_qk[:, 2048:4096], ktile))
            t_sc = emit_scores(0)
            exp_tok[0] = emit_exp(0, t_sc)
            if DEBUG and st == DEBUG_ST and p == 0:
                _w(SY, ve, exp_tok[0])
                t_dsc = stm.inc(SY.dma_start(dbg_sc, expS[0:128, 0:240]))
                _w(T, stm, t_dsc)
            for c in range(NCH):
                if c + 1 < NCH:
                    t_sc = emit_scores(c + 1)
                av_tok[c] = emit_attnv(c, exp_tok[c])
                if DEBUG and st == DEBUG_ST and p == 0 and c == 0:
                    nc.vector.wait_ge(pe.h, av_tok[c])
                    V.tensor_copy(dbg_av_sb[0:120, 0:65], psBa[0:120, 1024:1089])
                    t_dav = ve.inc(V.tensor_copy(dbg_av_sb[0:120, 65:130],
                                                 psBa[0:120, 1536:1601]))
                    _w(SY, ve, t_dav)
                    t_davd = stm.inc(SY.dma_start(dbg_av, dbg_av_sb[:, :]))
                if c + 1 < NCH:
                    exp_tok[c + 1] = emit_exp(c + 1, t_sc)
                emit_evac(c, av_tok[c])
            pair_free[pb] = max(av_free)

        if st == 1:
            ratm = abf(A_RATM, 1024, 40)
            rr40 = af32(A_RR, 16, 40)
            t_rv = (stm, max(v for _, v in t_runt_v))
            t_rqk = max(t_runt_qk)
            for p in range(8):
                pg = p % 2
                _w(T, ve, t_rqk)
                _w(T, se, sc_free[0])
                _w(T, se, sc_free[1])
                T.matmul(psBa[0:40, 0:40], k_runt[0:64, p * 40:(p + 1) * 40],
                         q_runt[0:64, p * 40:(p + 1) * 40], start=True, stop=True)
                mm = T.matmul(psBa[0:40, 512:552], k_runt[64:128, p * 40:(p + 1) * 40],
                              q_runt[64:128, p * 40:(p + 1) * 40], start=True, stop=True)
                t_sc = pe.inc(mm)
                _w(S, pe, t_sc)
                S.activation(expS[0:40, pg * 240:pg * 240 + 40],
                             psBa[0:40, 0:40], AF.Exp, scale=ATTS)
                t_exp = se.inc(S.activation(expS[0:40, pg * 240 + 40:pg * 240 + 80],
                                            psBa[0:40, 512:552], AF.Exp, scale=ATTS))
                sc_free[0] = t_exp
                sc_free[1] = t_exp
                _w(V, se, t_exp)
                V.tensor_mul(expS[0:40, pg * 240:pg * 240 + 40],
                             expS[0:40, pg * 240:pg * 240 + 40], amask_sb[0:40, 0:40])
                t_msk = ve.inc(V.tensor_mul(expS[0:40, pg * 240 + 40:pg * 240 + 80],
                                            expS[0:40, pg * 240 + 40:pg * 240 + 80],
                                            amask_sb[0:40, 0:40]))
                wd(T, t_rv)
                _w(T, ve, t_msk)
                _w(T, ve, av_free[0])
                _w(T, ve, av_free[1])
                T.matmul(psBa[0:40, 1024:1089], expS[0:40, pg * 240:pg * 240 + 40],
                         v_r40[0:40, (2 * p) * 65:(2 * p) * 65 + 65],
                         start=True, stop=True)
                mm = T.matmul(psBa[0:40, 1536:1601],
                              expS[0:40, pg * 240 + 40:pg * 240 + 80],
                              v_r40[0:40, (2 * p + 1) * 65:(2 * p + 1) * 65 + 65],
                              start=True, stop=True)
                t_av = pe.inc(mm)
                src2 = psBa[0:40, 1024:2048].rearrange("p (h x) -> p h x", h=2, x=512)
                _w(V, pe, t_av)
                V.tensor_copy(ratm[0:40, 2 * p * 64:2 * p * 64 + 128].rearrange(
                    "p (h x) -> p h x", h=2, x=64), src2[:, :, 0:64])
                t2 = ve.inc(V.tensor_copy(
                    rr40[0:40, 2 * p:2 * p + 2].rearrange("p (h x) -> p h x", h=2, x=1),
                    src2[:, :, 64:65]))
                av_free[0] = t2
                av_free[1] = t2

        if DEBUG and st == DEBUG_ST:
            _w(SY, ve, max(av_free))
            stm.inc(SY.dma_start(dbg_atm, abf(A_ATM, NCH * 1024)))
            stm.inc(SY.dma_start(dbg_r, r_all[:, :]))
        _w(G, ve, max(av_free))
        t_wo = dma_wo()
        xr_tok = [None] * NCH
        comb_tok = [0, 0]
        nrm_tok = [0, 0]
        last_rr = 0
        store_tok = [None] * NCH
        wps_tok = [0] * NCH
        tr_tok = [0] * NCH
        _w(SY, ve, max(av_free))
        for c0 in (0, 1):
            xr_tok[c0] = (xl[c0], xl[c0].inc(SY.dma_start(
                af32(A_XR[c0], 1024, 120),
                xin[base + c0 * 120:base + c0 * 120 + 120, :])))

        def ph5_stage_a(c):
            nonlocal last_rr
            pg = c % 2
            if c == 0:
                _w(S, ve, max(av_free))
            if c >= 2:
                _w(S, ve, nrm_tok[pg])
            _w(S, se, last_rr)
            t_ln = se.inc(S.activation(stat[0:120, 0:16],
                                       r_all[0:120, c * 16:c * 16 + 16], AF.Ln))
            _w(S, se, t_ln)
            t_rr = se.inc(S.activation(r_rec[0:120, pg * 16:pg * 16 + 16],
                                       stat[0:120, 0:16], AF.Exp, scale=-1.0))
            last_rr = t_rr
            nrm = abf(A_NRM[pg], 1024)
            _w(V, ve, max(av_free) if c == 0 else 0)
            _w(V, se, t_rr)
            for h in range(16):
                tn = V.tensor_scalar_mul(
                    nrm[0:120, h * 64:(h + 1) * 64],
                    atm[0:120, c * 1024 + h * 64:c * 1024 + (h + 1) * 64],
                    r_rec[0:120, pg * 16 + h:pg * 16 + h + 1])
            t_nrm = ve.inc(tn)
            nrm_tok[pg] = t_nrm
            _w(T, ve, t_nrm)
            _w(T, ve, tr_free[pg])
            for i in range(8):
                mm = T.matmul(psA_bf(pg, 960)[:, i * 120:(i + 1) * 120],
                              nrm[0:120, i * 128:(i + 1) * 128], identt[0:120, 0:120],
                              is_transpose=True, start=(i == 0), stop=(i == 7),
                              skip_group_check=True)
            t_tr = pe.inc(mm)
            tr_tok[c] = t_tr

        def ph5_stage_a2(c):
            pg = c % 2
            fmv = abf(A_FM[pg], 960)
            _w(V, pe, tr_tok[c])
            tr_free[pg] = ve.inc(V.tensor_copy(fmv, psA_bf(pg, 960)))
            t_fm = tr_free[pg]
            dstp = psBa[0:120, pg * 1024:pg * 1024 + 1024]
            wd(T, t_wo)
            _w(T, ve, t_fm)
            _w(T, ve, psB_free[pg])
            if c < 2:
                _w(T, se, max(sc_free))
            for kt in range(8):
                for j in range(2):
                    mm = T.matmul(
                        dstp[:, j * 512:(j + 1) * 512],
                        fmv[:, kt * 120:(kt + 1) * 120],
                        abf(A_WO, 8192)[:, kt * 1024 + j * 512:kt * 1024 + (j + 1) * 512],
                        start=(kt == 0), stop=(kt == 7))
                    if j == 1:
                        nc.mark_noldw()
            wps_tok[c] = pe.inc(mm)

        def ph5_stage_b(c):
            pg = c % 2
            dstp = psBa[0:120, pg * 1024:pg * 1024 + 1024]
            sv = s_tm.ap()[0:120, pg * 1024:pg * 1024 + 1024]
            _w(V, pe, wps_tok[c])
            wd(V, CT)
            if c >= 2:
                wd(V, store_tok[c - 2])
            _w(V, se, gtmp_free[pg])
            t_sadd = ve.inc(V.tensor_add(sv, dstp, boutb[0:120, :]))
            psB_free[pg] = t_sadd
            if DEBUG and st == DEBUG_ST:
                _w(SY, ve, t_sadd)
                stm.inc(SY.dma_start(dbg_s[c * 120:c * 120 + 120, :], sv))
                _w(V, stm, stm.n)
            _w(V, se, t_g_ready[st])
            _w(V, ve, t_sadd)
            t_smul = ve.inc(V.tensor_mul(sv, sv, g_buf[0:120, c * 1024:(c + 1) * 1024]))
            wd(V, xr_tok[c])
            _w(V, ve, t_smul)
            t_comb = ve.inc(V.tensor_add(sv, sv, af32(A_XR[pg], 1024, 120)))
            comb_tok[pg] = t_comb
            _w(SY, ve, t_comb)
            store_tok[c] = (sts[pg], sts[pg].inc(SY.dma_start(
                out_d[base + c * 120:base + c * 120 + 120, :], sv)))
            if c + 2 < NCH:
                xr_tok[c + 2] = (xl[pg], xl[pg].inc(SY.dma_start(
                    af32(A_XR[pg], 1024, 120),
                    xin[base + (c + 2) * 120:base + (c + 2) * 120 + 120, :])))

        ph5_stage_a(0)
        ph5_stage_a(1)
        for c in range(NCH):
            ph5_stage_a2(c)
            if c + 2 < NCH:
                ph5_stage_a(c + 2)
            if c >= 1:
                ph5_stage_b(c - 1)
        ph5_stage_b(NCH - 1)

        if st == 1:
            ratm = abf(A_RATM, 1024, 40)
            rr40 = af32(A_RR, 16, 40)
            gr40 = abf(A_RG, 1024, 40)
            xr40 = af32(A_RXR, 1024, 40)
            _w(V, ve, ve.n)
            t_z = ve.inc(V.memset(gr40[0:40, :], 0.0))
            _w(SY, se, t_g_ready[0])
            _w(SY, se, t_g_ready[1])
            _w(SY, ve, t_z)
            stm.inc(SY.dma_start(gr40[0:8, :], g_r0[0:8, :]))
            t12 = (stm, stm.inc(SY.dma_start(gr40[20:28, :], g_r1[0:8, :])))
            xl[0].inc(SY.dma_start(xr40[0:8, :], xin[2040:2048, :]))
            t34 = (xl[0], xl[0].inc(SY.dma_start(xr40[20:28, :], xin[4088:4096, :])))
            _w(S, ve, ve.n)
            _w(S, se, last_rr)
            t_ln = se.inc(S.activation(stat[0:40, 0:16], rr40[0:40, 0:16], AF.Ln))
            _w(S, se, t_ln)
            t_rr = se.inc(S.activation(r_rec[0:40, 0:16], stat[0:40, 0:16],
                                       AF.Exp, scale=-1.0))
            nrm = abf(A_NRM[0], 1024, 40)
            _w(V, se, t_rr)
            for h in range(16):
                tn = V.tensor_scalar_mul(nrm[0:40, h * 64:(h + 1) * 64],
                                         ratm[0:40, h * 64:(h + 1) * 64],
                                         r_rec[0:40, h:h + 1])
            t_nrm = ve.inc(tn)
            _w(T, ve, t_nrm)
            _w(T, ve, tr_free[0])
            for i in range(8):
                mm = T.matmul(psA_bf(0, 320)[:, i * 40:(i + 1) * 40],
                              nrm[0:40, i * 128:(i + 1) * 128], identt[0:40, 0:40],
                              is_transpose=True, start=(i == 0), stop=(i == 7),
                              skip_group_check=True)
            t_tr = pe.inc(mm)
            fmv = abf(A_FM[0], 320)
            _w(V, pe, t_tr)
            tr_free[0] = ve.inc(V.tensor_copy(fmv, psA_bf(0, 320)))
            dstp = psBa[0:40, 0:1024]
            wd(T, t_wo)
            _w(T, ve, tr_free[0])
            _w(T, ve, psB_free[0])
            for kt in range(8):
                for j in range(2):
                    mm = T.matmul(
                        dstp[:, j * 512:(j + 1) * 512],
                        fmv[:, kt * 40:(kt + 1) * 40],
                        abf(A_WO, 8192)[:, kt * 1024 + j * 512:kt * 1024 + (j + 1) * 512],
                        start=(kt == 0), stop=(kt == 7))
                    if j == 1:
                        nc.mark_noldw()
            t_wps = pe.inc(mm)
            sv = s_tm.ap()[0:40, 0:1024]
            _w(V, pe, t_wps)
            wd(V, t12)
            wd(V, store_tok[NCH - 1])
            wd(V, t34)
            wd(V, CT)
            t_a1 = ve.inc(V.tensor_add(sv, dstp, boutb[0:40, :]))
            _w(V, ve, t_a1)
            t_a2 = ve.inc(V.tensor_mul(sv, sv, gr40[0:40, :]))
            _w(V, ve, t_a2)
            t_comb = ve.inc(V.tensor_add(sv, sv, xr40[0:40, :]))
            _w(SY, ve, t_comb)
            stm.inc(SY.dma_start(out_d[2040:2048, :], sv[0:8, :]))
            stm.inc(SY.dma_start(out_d[4088:4096, :], sv[20:28, :]))

    assert ring_ci[0] == len(ring_plan) == len(ring_toks)
    for _sem in (sts[0], sts[1], stm):
        _w(SY, _sem, _sem.n)
    ctx.close()
    return nc


_NC_CACHE = None


def _get_nc():
    global _NC_CACHE
    if _NC_CACHE is None:
        _NC_CACHE = build_nc()
    return _NC_CACHE


def kernel(L0_attn_x, L0_attn_Wqkv, L0_attn_Wout, L0_attn_bout,
           L0_ffn_w_gate, L0_ffn_w_up, L0_ffn_w_down):
    x = np.asarray(L0_attn_x, dtype=np.float32)
    imgs = _weight_images(L0_attn_Wqkv, L0_attn_Wout,
                          np.asarray(L0_ffn_w_gate, dtype=np.int32),
                          np.asarray(L0_ffn_w_up, dtype=np.int32),
                          np.asarray(L0_ffn_w_down, dtype=np.int32),
                          L0_attn_bout)
    order = _perm_order()
    nc = _get_nc()
    in_maps = []
    for c in range(8):
        m = {"x_perm": np.ascontiguousarray(x[c][order]),
             "wqk": imgs["wqk"], "wv": imgs["wv"], "wo": imgs["wo"],
             "wg": imgs["wg"], "wu": imgs["wu"], "wd": imgs["wd"],
             "boutb": imgs["boutb"], "ident": imgs["ident"],
             "amask": imgs["amask"]}
        in_maps.append(m)
    res = run_bass_kernel_spmd(nc, in_maps, core_ids=list(range(8)))
    inv = np.empty_like(order)
    inv[order] = np.arange(NTOK)
    out = np.empty((8, NTOK, D), dtype=np.float32)
    for c in range(8):
        out[c] = np.asarray(res.results[c]["out_perm"])[inv]
    return out


# version 14
# speedup vs baseline: 1.1797x; 1.1797x over previous
import contextlib
import numpy as np
import orjson
import concourse.bass as bass
import concourse.mybir as mybir
from concourse.bass_utils import run_bass_kernel_spmd


class PBass(bass.Bass):

    def __init__(self, *a, **k):
        super().__init__(*a, **k)
        self.noldw_names = set()

    def mark_noldw(self):
        f = self.m.functions[-1]
        self.noldw_names.add(f.blocks[-1].instructions[-1].name)

    def to_json_bytes(self):
        b = super().to_json_bytes()
        if not self.noldw_names:
            return b
        j = orjson.loads(b)

        def walk(o):
            if isinstance(o, dict):
                if o.get('opcode') == 'Matmult' and o.get('name') in self.noldw_names:
                    o['ldweights'] = False
                for v in o.values():
                    walk(v)
            elif isinstance(o, list):
                for v in o:
                    walk(v)
        walk(j)
        return orjson.dumps(j)

F32 = mybir.dt.float32
BF16 = mybir.dt.bfloat16
I32 = mybir.dt.int32
AF = mybir.ActivationFunctionType
ALU = mybir.AluOpType
AX = mybir.AxisListType

D = 1024
NTOK = 4096
NCH = 17
STW = 2048
S127 = 1.0 / 127.0
ATTS = 0.125
EPS = 1e-5



def _perm_order():
    def chunk(c):
        toks = []
        for b in range(3):
            blk = 3 * c + b
            for par in (0, 1):
                for s in range(20):
                    toks.append(40 * blk + par + 2 * s)
        return toks

    order = []
    for c in range(17):
        order += chunk(c)
    order += [4080 + 2 * i for i in range(8)]
    for c in range(17, 34):
        order += chunk(c)
    order += [4081 + 2 * i for i in range(8)]
    assert len(order) == 4096 and len(set(order)) == 4096
    return np.asarray(order, dtype=np.int64)


def _weight_images(Wqkv, Wout, wg, wu, wd, bout):
    WqkvT = np.ascontiguousarray(np.asarray(Wqkv).T)
    WoutT = np.ascontiguousarray(np.asarray(Wout).T)
    WgT = np.ascontiguousarray(np.asarray(wg).T)
    WuT = np.ascontiguousarray(np.asarray(wu).T)
    WdT = np.ascontiguousarray(np.asarray(wd).T)
    wqk = np.ascontiguousarray(
        WqkvT[:, 0:2048].reshape(8, 128, 16, 128).transpose(2, 1, 0, 3).reshape(2048, 1024))
    wv = np.ascontiguousarray(
        WqkvT[:, 2048:3072].reshape(8, 128, 1024).transpose(1, 0, 2).reshape(128, 8192))
    wo = np.ascontiguousarray(
        WoutT.reshape(8, 128, 1024).transpose(1, 0, 2).reshape(128, 8192))

    def gu(WT):
        return np.ascontiguousarray(
            WT.reshape(8, 128, 24, 128).transpose(2, 1, 0, 3).reshape(3072, 1024))

    wgi, wui = gu(WgT), gu(WuT)
    wdi = np.ascontiguousarray(
        WdT.reshape(2, 12, 128, 1024).transpose(0, 2, 1, 3).reshape(256, 12288))
    boutb = np.ascontiguousarray(
        np.broadcast_to(np.asarray(bout, dtype=np.float32), (128, 1024)))
    ident = np.eye(128, dtype=np.float32)
    amask = np.zeros((120, 240), dtype=np.float32)
    for gi in range(6):
        amask[gi * 20:(gi + 1) * 20, gi * 20:(gi + 1) * 20] = 1.0
    amask[:, 120:240] = amask[:, 0:120]
    return dict(wqk=wqk, wv=wv, wo=wo, wg=wgi, wu=wui, wd=wdi,
                boutb=boutb, ident=ident, amask=amask)



class Sem:
    def __init__(self, ctx, nc, name, step):
        self.h = ctx.enter_context(nc.semaphore(name))
        self.step = step
        self.n = 0

    def inc(self, instr):
        instr.then_inc(self.h, self.step)
        self.n += self.step
        return self.n


DEBUG = False
DEBUG_ST = 0


def build_nc():
    ctx = contextlib.ExitStack()
    nc = PBass("TRN2", target_bir_lowering=False)
    if DEBUG:
        dbg_xn = nc.dram_tensor("dbg_xn", [128, 16384], BF16, kind="ExternalOutput").ap()
        dbg_g = nc.dram_tensor("dbg_g", [128, 17408], BF16, kind="ExternalOutput").ap()
        dbg_v = nc.dram_tensor("dbg_v", [128, 17680], BF16, kind="ExternalOutput").ap()
        dbg_atm = nc.dram_tensor("dbg_atm", [128, 17408], BF16, kind="ExternalOutput").ap()
        dbg_r = nc.dram_tensor("dbg_r", [128, 272], F32, kind="ExternalOutput").ap()
        dbg_s = nc.dram_tensor("dbg_s", [2048, 1024], F32, kind="ExternalOutput").ap()
        dbg_qk = nc.dram_tensor("dbg_qk", [128, 4096], BF16, kind="ExternalOutput").ap()
        dbg_sc = nc.dram_tensor("dbg_sc", [128, 240], BF16, kind="ExternalOutput").ap()
        dbg_av = nc.dram_tensor("dbg_av", [128, 130], F32, kind="ExternalOutput").ap()
        dbg_av_sb = ctx.enter_context(nc.sbuf_tensor("dbg_av_sb", [128, 130], F32))

    xin = nc.dram_tensor("x_perm", [NTOK, D], F32, kind="ExternalInput").ap()
    wqk_d = nc.dram_tensor("wqk", [2048, 1024], F32, kind="ExternalInput").ap()
    wv_d = nc.dram_tensor("wv", [128, 8192], F32, kind="ExternalInput").ap()
    wo_d = nc.dram_tensor("wo", [128, 8192], F32, kind="ExternalInput").ap()
    wg_d = nc.dram_tensor("wg", [3072, 1024], I32, kind="ExternalInput").ap()
    wu_d = nc.dram_tensor("wu", [3072, 1024], I32, kind="ExternalInput").ap()
    wd_d = nc.dram_tensor("wd", [256, 12288], I32, kind="ExternalInput").ap()
    boutb_d = nc.dram_tensor("boutb", [128, 1024], F32, kind="ExternalInput").ap()
    ident_d = nc.dram_tensor("ident", [128, 128], F32, kind="ExternalInput").ap()
    amask_d = nc.dram_tensor("amask", [120, 240], F32, kind="ExternalInput").ap()
    out_d = nc.dram_tensor("out_perm", [NTOK, D], F32, kind="ExternalOutput").ap()

    xn_fm = ctx.enter_context(nc.sbuf_tensor("xn_fm", [128, 8 * STW], BF16))
    g_buf = ctx.enter_context(nc.sbuf_tensor("g_buf", [128, NCH * 1024], BF16))
    g_r0 = ctx.enter_context(nc.sbuf_tensor("g_r0", [8, 1024], BF16))
    g_r1 = ctx.enter_context(nc.sbuf_tensor("g_r1", [8, 1024], BF16))
    v_r40 = ctx.enter_context(nc.sbuf_tensor("v_r40", [40, 1040], BF16))
    v_rtmp = ctx.enter_context(nc.sbuf_tensor("v_rtmp", [8, 1040], BF16))
    q_runt = ctx.enter_context(nc.sbuf_tensor("q_runt", [128, 320], BF16))
    k_runt = ctx.enter_context(nc.sbuf_tensor("k_runt", [128, 320], BF16))
    stat = ctx.enter_context(nc.sbuf_tensor("stat", [128, 16], F32))
    eps_t = ctx.enter_context(nc.sbuf_tensor("eps_t", [128, 1], F32))
    expS = ctx.enter_context(nc.sbuf_tensor("expS", [128, 480], BF16))
    r_all = ctx.enter_context(nc.sbuf_tensor("r_all", [128, NCH * 16], F32))
    r_rec = ctx.enter_context(nc.sbuf_tensor("r_rec", [128, 32], F32))
    s_tm = ctx.enter_context(nc.sbuf_tensor("s_tm", [128, 2048], F32))
    boutb = ctx.enter_context(nc.sbuf_tensor("boutb_sb", [128, 1024], F32))
    identt = ctx.enter_context(nc.sbuf_tensor("identt", [128, 128], BF16))
    amask_sb = ctx.enter_context(nc.sbuf_tensor("amask_sb", [120, 240], BF16))
    arena = ctx.enter_context(nc.sbuf_tensor("arena", [128, 98304], mybir.dt.int8))
    ar = arena.ap()

    def abf(off, cols, rows=128):
        return ar[0:rows, off:off + 2 * cols].bitcast(BF16)

    def af32(off, cols, rows=128):
        return ar[0:rows, off:off + 4 * cols].bitcast(F32)

    A_XTM = [0, 4096]
    A_XNTM = [8192, 10240]
    A_SQ = 12288
    A_H = [k * 4096 for k in range(12)]
    A_R = [49152, 73728]
    A_WV = 73728
    A_VAUG = 16384
    A_PAIR = [0, 4096, 8192, 12288]
    A_WTILE = [86560, 88608]
    A_ATM = 51744
    A_XR = [16384, 20480]
    A_NRM = [24576, 26624]
    A_FM = [28672, 30592]
    A_WO = 32512
    A_RATM = 0
    A_RR = 2048
    A_RG = 4160
    A_RXR = 8256

    psA = ctx.enter_context(nc.psum_tensor("psA", [128, 2048], F32))
    psB = ctx.enter_context(nc.psum_tensor("psB", [128, 2048], F32))
    psAa, psBa = psA.ap(), psB.ap()

    def psA_bf(bank, cols, rows=128):
        return psAa[0:rows, bank * 512:bank * 512 + (cols + 1) // 2].bitcast(BF16)

    xl = [Sem(ctx, nc, "xlA", 16), Sem(ctx, nc, "xlB", 16)]
    sts = [Sem(ctx, nc, "stA", 16), Sem(ctx, nc, "stB", 16)]
    stm = Sem(ctx, nc, "stm", 16)
    wr = [Sem(ctx, nc, "wr0", 16)]
    wts = [Sem(ctx, nc, "wt0", 16), Sem(ctx, nc, "wt1", 16),
           Sem(ctx, nc, "wt2", 16)]
    wvo = Sem(ctx, nc, "wvo", 16)
    cst = Sem(ctx, nc, "cst", 16)
    pe = Sem(ctx, nc, "pe", 1)
    ve = Sem(ctx, nc, "ve", 1)
    se = Sem(ctx, nc, "se", 1)
    ge = Sem(ctx, nc, "ge", 1)

    def _w(eng, sem, tok):
        if tok <= 0:
            return
        assert tok <= sem.n, "wait on future token"
        eng.wait_ge(sem.h, tok)

    def wd(eng, tok):
        if tok is None:
            return
        sem, val = tok
        _w(eng, sem, val)

    T, V, S, SY, G = nc.tensor, nc.vector, nc.scalar, nc.sync, nc.gpsimd

    cst.inc(G.dma_start(identt[:, :], ident_d))
    cst.inc(G.dma_start(boutb[:, :], boutb_d))
    cst.inc(G.dma_start(amask_sb[:, :], amask_d))
    CT = (cst, cst.n)
    t_vz = ve.inc(V.memset(v_r40[0:40, :], 0.0))
    _w(V, ve, t_vz)
    t_vr_ones = ve.inc(V.memset(v_r40[0:40, 64:1040:65], 1.0))
    V.memset(q_runt[:, :], 0.0)
    t_qkr_z = ve.inc(V.memset(k_runt[:, :], 0.0))
    t_eps = ve.inc(V.memset(eps_t[:, :], EPS))

    RING_OFF = [92160, 94208, 96256]
    ring_plan = []
    for st_ in range(2):
        for hf_ in range(2):
            for ml_ in range(12):
                ring_plan.append(("u", hf_ * 12 + ml_))
            for ml_ in range(12):
                ring_plan.append(("g", hf_ * 12 + ml_))
        for p_ in range(8):
            ring_plan.append(("qk", p_))
            ring_plan.append(("qk", 8 + p_))
    ring_src = {"u": wu_d, "g": wg_d, "qk": wqk_d}
    ring_toks = []
    ring_free_pe = [0, 0, 0]
    ring_ci = [0]

    def ring_issue():
        i = len(ring_toks)
        if i >= len(ring_plan):
            return
        kind, row = ring_plan[i]
        sl = i % 3
        _w(G, pe, ring_free_pe[sl])
        ring_toks.append((wts[sl], wts[sl].inc(G.dma_start(
            abf(RING_OFF[sl], 1024), ring_src[kind][row * 128:(row + 1) * 128, :]))))

    def ring_take():
        i = ring_ci[0]
        ring_ci[0] += 1
        assert i < len(ring_toks), "ring item not issued yet"
        return i, ring_toks[i], abf(RING_OFF[i % 3], 1024)

    def ring_done(i, pe_tok):
        ring_free_pe[i % 3] = pe_tok
        ring_issue()

    for _ in range(3):
        ring_issue()

    A_WD = 49152

    def dma_wd(hf_):
        last = 0
        for kl in range(12):
            last = wr[0].inc(G.dma_start(
                abf(A_WD + kl * 2048, 1024),
                wd_d[hf_ * 128:hf_ * 128 + 128, kl * 1024:(kl + 1) * 1024]))
        return (wr[0], last)

    t_wd_tok = dma_wd(0)

    def dma_wv():
        last = 0
        for _q in range(4):
            last = wvo.inc(G.dma_start(abf(A_WV + _q * 4096, 2048),
                                       wv_d[:, _q * 2048:(_q + 1) * 2048]))
        return (wvo, last)

    def dma_wo():
        last = 0
        for _q in range(4):
            last = wvo.inc(G.dma_start(abf(A_WO + _q * 4096, 2048),
                                       wo_d[:, _q * 2048:(_q + 1) * 2048]))
        return (wvo, last)

    t_wv_tok = dma_wv()

    psA_free = [0, 0]
    psB_free = [0, 0]
    tr_free = [0, 0]
    sc_free = [0, 0]
    av_free = [0, 0]

    t_g_ready = [0, 0]
    t_runt_v = []
    t_runt_qk = []
    ident_rdy_waited = [False, False, False]

    for st in range(2):
        base = st * STW

        xld_toks = []
        t0 = (xl[0], xl[0].inc(SY.dma_start(af32(A_XTM[0], 1024, 120),
                                            xin[base:base + 120, :])))
        xld_toks.append(t0)
        apply_tok = [0, 0]
        sq_tok = [0, 0]
        var_tok = [0, 0]
        rstd_tok = [0, 0]
        tr_pe = [0, 0]
        for ci in range(NCH + 1):
            w = 120 if ci < NCH else 8
            pg = ci % 2
            if ci < NCH:
                nw = 120 if ci + 1 < NCH else 8
                off = base + (ci + 1) * 120 if ci + 1 < NCH else base + 2040
                npg = (ci + 1) % 2
                _w(SY, ve, apply_tok[npg])
                _w(SY, se, sq_tok[npg])
                xld_toks.append((xl[npg], xl[npg].inc(SY.dma_start(
                    af32(A_XTM[npg], 1024, nw), xin[off:off + nw, :]))))
            xt = af32(A_XTM[pg], 1024, w)
            sc = 8 * pg
            wd(V, xld_toks[ci])
            _w(V, ve, var_tok[pg])
            t_sum = ve.inc(V.tensor_reduce(stat[0:w, sc:sc + 1], xt, AX.X, ALU.add))
            wd(S, xld_toks[ci])
            _w(S, ve, var_tok[pg])
            t_sq = se.inc(S.activation(abf(A_SQ + pg * 2048, 1024, w), xt, AF.Square,
                                       accum_out=stat[0:w, sc + 1:sc + 2]))
            sq_tok[pg] = t_sq
            _w(V, ve, t_sum)
            t_m = ve.inc(V.tensor_scalar_mul(stat[0:w, sc + 2:sc + 3],
                                             stat[0:w, sc:sc + 1], 1.0 / D))
            _w(V, ve, t_m)
            t_msq = ve.inc(V.tensor_mul(stat[0:w, sc + 3:sc + 4],
                                        stat[0:w, sc + 2:sc + 3], stat[0:w, sc + 2:sc + 3]))
            _w(V, se, t_sq)
            _w(V, ve, t_msq)
            t_var = ve.inc(V.scalar_tensor_tensor(
                stat[0:w, sc + 4:sc + 5], stat[0:w, sc + 1:sc + 2], 1.0 / D,
                stat[0:w, sc + 3:sc + 4], ALU.mult, ALU.subtract))
            var_tok[pg] = t_var
            _w(S, ve, t_var)
            _w(S, ve, t_eps)
            _w(S, se, rstd_tok[pg])
            t_ln = se.inc(S.activation(stat[0:w, sc + 5:sc + 6],
                                       stat[0:w, sc + 4:sc + 5], AF.Ln,
                                       bias=eps_t[0:w, 0:1]))
            _w(S, se, t_ln)
            t_rstd = se.inc(S.activation(stat[0:w, sc + 6:sc + 7],
                                         stat[0:w, sc + 5:sc + 6], AF.Exp, scale=-0.5))
            rstd_tok[pg] = t_rstd
            xnt = abf(A_XNTM[pg], 1024, w)
            _w(V, se, t_rstd)
            _w(V, pe, tr_pe[pg])
            t_apply = ve.inc(V.tensor_scalar(xnt, xt, stat[0:w, sc + 2:sc + 3],
                                             stat[0:w, sc + 6:sc + 7],
                                             ALU.subtract, ALU.mult))
            apply_tok[pg] = t_apply
            tb = pg
            _w(T, ve, t_apply)
            _w(T, ve, tr_free[tb])
            if not ident_rdy_waited[0]:
                wd(T, CT)
                ident_rdy_waited[0] = True
            for i in range(8):
                mm = T.matmul(psA_bf(tb, 8 * w)[:, i * w:(i + 1) * w],
                              xnt[:, i * 128:(i + 1) * 128], identt[0:w, 0:w],
                              is_transpose=True, start=(i == 0), stop=(i == 7),
                              skip_group_check=True)
            t_tr = pe.inc(mm)
            tr_pe[tb] = t_tr
            colb = ci * 120 if ci < NCH else 2040
            dst = xn_fm[:, 0:8 * STW].rearrange("p (k c) -> p k c", k=8, c=STW)[:, :, colb:colb + w]
            src = psA_bf(tb, 8 * w).rearrange("p (k c) -> p k c", k=8, c=w)
            _w(V, pe, t_tr)
            tr_free[tb] = ve.inc(V.tensor_copy(dst, src))
        ve_xn = max(tr_free)
        if DEBUG and st == DEBUG_ST:
            _w(SY, ve, ve_xn)
            stm.inc(SY.dma_start(dbg_xn, xn_fm[:, :]))

        if st == 1:
            _w(G, ve, ve.n)
            t_wd_tok = dma_wd(0)
            t_wv_tok = dma_wv()
        for _sem in (sts[0], sts[1], stm):
            _w(S, _sem, _sem.n)
            _w(V, _sem, _sem.n)
        gtmp_free = [0, 0]
        gpart_tok = {}
        t_down_last = 0
        fill_free = [max(psA_free), max(psB_free)]
        sg_free = [0, 0]
        for hf in range(2):
            h_tok = {}
            fill_free[0] = max(fill_free[0], max(psA_free))
            fill_free[1] = max(fill_free[1], max(psB_free))
            for ml in range(12):
                ri, rtok, rview = ring_take()
                pp = ml % 2
                ps = psAa if pp == 0 else psBa
                wd(T, rtok)
                _w(T, ve, fill_free[pp])
                if ml < 2:
                    _w(T, ve, ve_xn)
                for k in range(8):
                    for j in range(4):
                        mm = T.matmul(
                            ps[:, j * 512:(j + 1) * 512],
                            rview[:, k * 128:(k + 1) * 128],
                            xn_fm[:, k * STW + j * 512:k * STW + (j + 1) * 512],
                            start=(k == 0), stop=(k == 7))
                        if j > 0:
                            nc.mark_noldw()
                t_mm = pe.inc(mm)
                _w(V, pe, t_mm)
                fill_free[pp] = ve.inc(V.tensor_copy(abf(A_H[ml], 2048),
                                                     ps[:, 0:2048]))
                h_tok[ml] = fill_free[pp]
                ring_done(ri, t_mm)

            for ml in range(12):
                ri, rtok, rview = ring_take()
                pp = ml % 2
                ps = psAa if pp == 0 else psBa
                wd(T, rtok)
                _w(T, ve, fill_free[pp])
                for k in range(8):
                    for j in range(4):
                        mm = T.matmul(
                            ps[:, j * 512:(j + 1) * 512],
                            rview[:, k * 128:(k + 1) * 128],
                            xn_fm[:, k * STW + j * 512:k * STW + (j + 1) * 512],
                            start=(k == 0), stop=(k == 7))
                        if j > 0:
                            nc.mark_noldw()
                t_mm = pe.inc(mm)
                sgp = s_tm.ap()[:, pp * 1024:(pp + 1) * 1024].bitcast(BF16)
                _w(S, pe, t_mm)
                _w(S, ve, sg_free[pp])
                t_silu = se.inc(S.activation(sgp, ps[:, 0:2048], AF.Silu, scale=S127))
                hview = abf(A_H[ml], 2048)
                _w(V, se, t_silu)
                _w(V, ve, h_tok[ml])
                tt = ve.inc(V.tensor_mul(hview, hview, sgp))
                fill_free[pp] = tt
                sg_free[pp] = tt
                ring_done(ri, t_mm)
            psA_free = [fill_free[0], fill_free[0]]
            psB_free = [fill_free[1], fill_free[1]]

            t_wd = t_wd_tok
            Rd = abf(A_WD, 12288)
            h_ready = max(psA_free)
            for ci in range(NCH + 1):
                w = 120 if ci < NCH else 8
                colb = ci * 120 if ci < NCH else 2040
                pp = ci % 2
                dstp = psBa[0:w, pp * 1024:pp * 1024 + 1024]
                wd(T, t_wd)
                _w(T, ve, psB_free[pp])
                if ci == 0:
                    _w(T, ve, h_ready)
                    if st == 0 and hf == 0:
                        _w(T, se, max(sc_free))
                for kl in range(12):
                    for j in range(2):
                        mm = T.matmul(
                            dstp[:, j * 512:(j + 1) * 512],
                            abf(A_H[kl], 2048)[:, colb:colb + w],
                            Rd[:, kl * 1024 + j * 512:kl * 1024 + (j + 1) * 512],
                            start=(kl == 0), stop=(kl == 11))
                        if j == 1:
                            nc.mark_noldw()
                t_mm = pe.inc(mm)
                t_down_last = t_mm
                if ci < NCH:
                    gdst = g_buf[0:w, ci * 1024:(ci + 1) * 1024]
                else:
                    gdst = (g_r0 if st == 0 else g_r1)[0:w, :]
                if hf == 0:
                    _w(V, pe, t_mm)
                    psB_free[pp] = ve.inc(V.tensor_scalar_mul(gdst, dstp, S127 * S127))
                    gpart_tok[ci] = psB_free[pp]
                else:
                    gtmp = s_tm.ap()[0:w, pp * 1024:pp * 1024 + 1024]
                    _w(V, pe, t_mm)
                    _w(V, se, gtmp_free[pp])
                    _w(V, ve, gpart_tok[ci])
                    t_gt = ve.inc(V.scalar_tensor_tensor(
                        gtmp, dstp, S127 * S127, gdst, ALU.mult, ALU.add))
                    psB_free[pp] = t_gt
                    _w(S, ve, t_gt)
                    t_g = se.inc(S.activation(gdst, gtmp, AF.Silu))
                    gtmp_free[pp] = t_g
            if hf == 0:
                _w(G, pe, t_down_last)
                t_wd_tok = dma_wd(1)
            t_g_ready[st] = se.n
        if DEBUG and st == DEBUG_ST:
            _w(SY, se, t_g_ready[st])
            stm.inc(SY.dma_start(dbg_g, g_buf[:, :]))

        t_wv = t_wv_tok
        vaug = abf(A_VAUG, NCH * 1040)
        _w(V, ve, ve.n)
        _w(V, pe, pe.n)
        t_ones = ve.inc(V.memset(
            vaug[0:120, :].rearrange("p (c h g) -> p c h g",
                                     c=NCH, h=16, g=65)[:, :, :, 64:65], 1.0))
        for ci in range(NCH + 1):
            w = 120 if ci < NCH else 8
            colb = ci * 120 if ci < NCH else 2040
            pp = ci % 2
            dstp = psBa[0:w, pp * 1024:pp * 1024 + 1024]
            wd(T, t_wv)
            _w(T, ve, psB_free[pp])
            for k in range(8):
                for j in range(2):
                    mm = T.matmul(
                        dstp[:, j * 512:(j + 1) * 512],
                        xn_fm[:, k * STW + colb:k * STW + colb + w],
                        abf(A_WV, 8192)[:, k * 1024 + j * 512:k * 1024 + (j + 1) * 512],
                        start=(k == 0), stop=(k == 7))
                    if j == 1:
                        nc.mark_noldw()
            t_mm = pe.inc(mm)
            _w(V, pe, t_mm)
            if ci < NCH:
                dst = vaug[0:120, ci * 1040:(ci + 1) * 1040].rearrange(
                    "p (h g) -> p h g", h=16, g=65)[:, :, 0:64]
                _w(V, ve, t_ones)
                psB_free[pp] = ve.inc(V.tensor_copy(
                    dst, dstp.rearrange("p (h g) -> p h g", h=16, g=64)))
            elif st == 0:
                dst = v_r40[0:8, :].rearrange("p (h g) -> p h g", h=16, g=65)[:, :, 0:64]
                _w(V, ve, t_vr_ones)
                psB_free[pp] = ve.inc(V.tensor_copy(
                    dst, dstp.rearrange("p (h g) -> p h g", h=16, g=64)))
            else:
                tv1 = ve.inc(V.memset(v_rtmp[0:8, :], 0.0))
                _w(V, ve, tv1)
                tv2 = ve.inc(V.memset(v_rtmp[0:8, 64:1040:65], 1.0))
                dst = v_rtmp[0:8, :].rearrange("p (h g) -> p h g", h=16, g=65)[:, :, 0:64]
                _w(V, ve, tv2)
                tv3 = ve.inc(V.tensor_copy(
                    dst, dstp.rearrange("p (h g) -> p h g", h=16, g=64)))
                psB_free[pp] = tv3
                _w(SY, ve, tv3)
                _w(SY, ve, t_vr_ones)
                t_runt_v.append((stm, stm.inc(SY.dma_start(v_r40[20:28, :],
                                                           v_rtmp[0:8, :]))))
        t_vaug = max(psB_free)
        if DEBUG and st == DEBUG_ST:
            _w(SY, ve, t_vaug)
            stm.inc(SY.dma_start(dbg_v, abf(A_VAUG, NCH * 1040)))

        pair_free = [0, 0]
        atm = abf(A_ATM, NCH * 1024)
        qk_rdy_of = [0] * 9

        def gen_fills(p):
            pb = p % 2
            for iqk in (0, 1):
                ri, rtok, rview = ring_take()
                tile = abf(A_PAIR[2 * pb + iqk], 2048)
                t_mm = 0
                for ch in range(2):
                    pp = ch
                    dstp = psAa[:, pp * 1024:(pp + 1) * 1024]
                    wd(T, rtok)
                    _w(T, ve, psA_free[pp])
                    if p >= 2 and ch == 0 and iqk == 0:
                        _w(T, ve, pair_free[pb])
                    for k in range(8):
                        for j in range(2):
                            mm = T.matmul(
                                dstp[:, j * 512:(j + 1) * 512],
                                rview[:, k * 128:(k + 1) * 128],
                                xn_fm[:, k * STW + ch * 1024 + j * 512:
                                      k * STW + ch * 1024 + (j + 1) * 512],
                                start=(k == 0), stop=(k == 7))
                            if j == 1:
                                nc.mark_noldw()
                    t_mm = pe.inc(mm)
                    _w(V, pe, t_mm)
                    psA_free[pp] = ve.inc(V.tensor_copy(
                        tile[:, ch * 1024:(ch + 1) * 1024], dstp))
                    if not (iqk == 1 and ch == 1):
                        yield
                ring_done(ri, t_mm)
            qk_rdy_of[p] = max(psA_free)

        for _ in gen_fills(0):
            pass
        for p in range(8):
            pb = p % 2
            filler = gen_fills(p + 1) if p + 1 < 8 else iter(())
            qk_rdy = qk_rdy_of[p]
            qtile = abf(A_PAIR[2 * pb + 0], 2048)
            ktile = abf(A_PAIR[2 * pb + 1], 2048)

            rc = 0 if st == 0 else 20
            _w(V, ve, t_qkr_z)
            _w(V, ve, qk_rdy)
            V.tensor_copy(q_runt[:, p * 40 + rc:p * 40 + rc + 8], qtile[:, 2040:2048])
            t_rk = ve.inc(V.tensor_copy(k_runt[:, p * 40 + rc:p * 40 + rc + 8],
                                        ktile[:, 2040:2048]))
            t_runt_qk.append(t_rk)

            exp_tok = [0] * NCH
            av_tok = [0] * NCH

            def emit_scores(c):
                cb = c * 120
                _w(T, ve, qk_rdy)
                _w(T, se, sc_free[0])
                _w(T, se, sc_free[1])
                T.matmul(psBa[0:120, 0:120], ktile[0:64, cb:cb + 120],
                         qtile[0:64, cb:cb + 120], start=True, stop=True)
                mm = T.matmul(psBa[0:120, 512:632], ktile[64:128, cb:cb + 120],
                              qtile[64:128, cb:cb + 120], start=True, stop=True)
                return pe.inc(mm)

            def emit_exp(c, t_sc):
                pg = c % 2
                _w(S, pe, t_sc)
                if c >= 2:
                    _w(S, pe, av_tok[c - 2])
                S.activation(expS[0:120, pg * 240:pg * 240 + 120],
                             psBa[0:120, 0:120], AF.Exp, scale=ATTS)
                t1 = se.inc(S.activation(expS[0:120, pg * 240 + 120:pg * 240 + 240],
                                         psBa[0:120, 512:632], AF.Exp, scale=ATTS))
                sc_free[0] = t1
                sc_free[1] = t1
                _w(V, se, t1)
                wd(V, CT)
                t2 = ve.inc(V.tensor_mul(expS[0:120, pg * 240:pg * 240 + 240],
                                         expS[0:120, pg * 240:pg * 240 + 240],
                                         amask_sb[0:120, :]))
                return t2

            def emit_attnv(c, t_msk):
                pg = c % 2
                if p == 0 and c == 0:
                    _w(T, ve, t_vaug)
                _w(T, ve, t_msk)
                _w(T, ve, av_free[0])
                _w(T, ve, av_free[1])
                T.matmul(psBa[0:120, 1024:1089], expS[0:120, pg * 240:pg * 240 + 120],
                         vaug[0:120, c * 1040 + 2 * p * 65:c * 1040 + 2 * p * 65 + 65],
                         start=True, stop=True)
                mm = T.matmul(psBa[0:120, 1536:1601],
                              expS[0:120, pg * 240 + 120:pg * 240 + 240],
                              vaug[0:120, c * 1040 + (2 * p + 1) * 65:
                                   c * 1040 + (2 * p + 1) * 65 + 65],
                              start=True, stop=True)
                return pe.inc(mm)

            def emit_evac(c, t_av):
                src = psBa[0:120, 1024:2048].rearrange("p (h x) -> p h x", h=2, x=512)
                _w(V, pe, t_av)
                V.tensor_copy(
                    atm[0:120, c * 1024 + 2 * p * 64:c * 1024 + 2 * p * 64 + 128]
                    .rearrange("p (h x) -> p h x", h=2, x=64),
                    src[:, :, 0:64])
                t2 = ve.inc(V.tensor_copy(
                    r_all[0:120, c * 16 + 2 * p:c * 16 + 2 * p + 2]
                    .rearrange("p (h x) -> p h x", h=2, x=1),
                    src[:, :, 64:65]))
                av_free[0] = t2
                av_free[1] = t2
                return t2

            if DEBUG and st == DEBUG_ST and p == 0:
                _w(SY, ve, qk_rdy)
                stm.inc(SY.dma_start(dbg_qk[:, 0:2048], qtile))
                stm.inc(SY.dma_start(dbg_qk[:, 2048:4096], ktile))
            t_sc = emit_scores(0)
            exp_tok[0] = emit_exp(0, t_sc)
            if DEBUG and st == DEBUG_ST and p == 0:
                _w(SY, ve, exp_tok[0])
                t_dsc = stm.inc(SY.dma_start(dbg_sc, expS[0:128, 0:240]))
                _w(T, stm, t_dsc)
            for c in range(NCH):
                if c + 1 < NCH:
                    t_sc = emit_scores(c + 1)
                if c in (3, 7, 11, 15):
                    next(filler, None)
                av_tok[c] = emit_attnv(c, exp_tok[c])
                if DEBUG and st == DEBUG_ST and p == 0 and c == 0:
                    nc.vector.wait_ge(pe.h, av_tok[c])
                    V.tensor_copy(dbg_av_sb[0:120, 0:65], psBa[0:120, 1024:1089])
                    t_dav = ve.inc(V.tensor_copy(dbg_av_sb[0:120, 65:130],
                                                 psBa[0:120, 1536:1601]))
                    _w(SY, ve, t_dav)
                    t_davd = stm.inc(SY.dma_start(dbg_av, dbg_av_sb[:, :]))
                if c + 1 < NCH:
                    exp_tok[c + 1] = emit_exp(c + 1, t_sc)
                emit_evac(c, av_tok[c])
            for _ in filler:
                pass
            pair_free[pb] = max(av_free)

        if st == 1:
            ratm = abf(A_RATM, 1024, 40)
            rr40 = af32(A_RR, 16, 40)
            t_rv = (stm, max(v for _, v in t_runt_v))
            t_rqk = max(t_runt_qk)
            for p in range(8):
                pg = p % 2
                _w(T, ve, t_rqk)
                _w(T, se, sc_free[0])
                _w(T, se, sc_free[1])
                T.matmul(psBa[0:40, 0:40], k_runt[0:64, p * 40:(p + 1) * 40],
                         q_runt[0:64, p * 40:(p + 1) * 40], start=True, stop=True)
                mm = T.matmul(psBa[0:40, 512:552], k_runt[64:128, p * 40:(p + 1) * 40],
                              q_runt[64:128, p * 40:(p + 1) * 40], start=True, stop=True)
                t_sc = pe.inc(mm)
                _w(S, pe, t_sc)
                S.activation(expS[0:40, pg * 240:pg * 240 + 40],
                             psBa[0:40, 0:40], AF.Exp, scale=ATTS)
                t_exp = se.inc(S.activation(expS[0:40, pg * 240 + 40:pg * 240 + 80],
                                            psBa[0:40, 512:552], AF.Exp, scale=ATTS))
                sc_free[0] = t_exp
                sc_free[1] = t_exp
                _w(V, se, t_exp)
                V.tensor_mul(expS[0:40, pg * 240:pg * 240 + 40],
                             expS[0:40, pg * 240:pg * 240 + 40], amask_sb[0:40, 0:40])
                t_msk = ve.inc(V.tensor_mul(expS[0:40, pg * 240 + 40:pg * 240 + 80],
                                            expS[0:40, pg * 240 + 40:pg * 240 + 80],
                                            amask_sb[0:40, 0:40]))
                wd(T, t_rv)
                _w(T, ve, t_msk)
                _w(T, ve, av_free[0])
                _w(T, ve, av_free[1])
                T.matmul(psBa[0:40, 1024:1089], expS[0:40, pg * 240:pg * 240 + 40],
                         v_r40[0:40, (2 * p) * 65:(2 * p) * 65 + 65],
                         start=True, stop=True)
                mm = T.matmul(psBa[0:40, 1536:1601],
                              expS[0:40, pg * 240 + 40:pg * 240 + 80],
                              v_r40[0:40, (2 * p + 1) * 65:(2 * p + 1) * 65 + 65],
                              start=True, stop=True)
                t_av = pe.inc(mm)
                src2 = psBa[0:40, 1024:2048].rearrange("p (h x) -> p h x", h=2, x=512)
                _w(V, pe, t_av)
                V.tensor_copy(ratm[0:40, 2 * p * 64:2 * p * 64 + 128].rearrange(
                    "p (h x) -> p h x", h=2, x=64), src2[:, :, 0:64])
                t2 = ve.inc(V.tensor_copy(
                    rr40[0:40, 2 * p:2 * p + 2].rearrange("p (h x) -> p h x", h=2, x=1),
                    src2[:, :, 64:65]))
                av_free[0] = t2
                av_free[1] = t2

        if DEBUG and st == DEBUG_ST:
            _w(SY, ve, max(av_free))
            stm.inc(SY.dma_start(dbg_atm, abf(A_ATM, NCH * 1024)))
            stm.inc(SY.dma_start(dbg_r, r_all[:, :]))
        _w(G, ve, max(av_free))
        t_wo = dma_wo()
        xr_tok = [None] * NCH
        comb_tok = [0, 0]
        nrm_tok = [0, 0]
        last_rr = 0
        store_tok = [None] * NCH
        wps_tok = [0] * NCH
        tr_tok = [0] * NCH
        _w(SY, ve, max(av_free))
        for c0 in (0, 1):
            xr_tok[c0] = (xl[c0], xl[c0].inc(SY.dma_start(
                af32(A_XR[c0], 1024, 120),
                xin[base + c0 * 120:base + c0 * 120 + 120, :])))

        def ph5_stage_a(c):
            nonlocal last_rr
            pg = c % 2
            if c == 0:
                _w(S, ve, max(av_free))
            if c >= 2:
                _w(S, ve, nrm_tok[pg])
            _w(S, se, last_rr)
            t_ln = se.inc(S.activation(stat[0:120, 0:16],
                                       r_all[0:120, c * 16:c * 16 + 16], AF.Ln))
            _w(S, se, t_ln)
            t_rr = se.inc(S.activation(r_rec[0:120, pg * 16:pg * 16 + 16],
                                       stat[0:120, 0:16], AF.Exp, scale=-1.0))
            last_rr = t_rr
            nrm = abf(A_NRM[pg], 1024)
            _w(V, ve, max(av_free) if c == 0 else 0)
            _w(V, se, t_rr)
            for h in range(16):
                tn = V.tensor_scalar_mul(
                    nrm[0:120, h * 64:(h + 1) * 64],
                    atm[0:120, c * 1024 + h * 64:c * 1024 + (h + 1) * 64],
                    r_rec[0:120, pg * 16 + h:pg * 16 + h + 1])
            t_nrm = ve.inc(tn)
            nrm_tok[pg] = t_nrm
            _w(T, ve, t_nrm)
            _w(T, ve, tr_free[pg])
            for i in range(8):
                mm = T.matmul(psA_bf(pg, 960)[:, i * 120:(i + 1) * 120],
                              nrm[0:120, i * 128:(i + 1) * 128], identt[0:120, 0:120],
                              is_transpose=True, start=(i == 0), stop=(i == 7),
                              skip_group_check=True)
            t_tr = pe.inc(mm)
            tr_tok[c] = t_tr

        def ph5_stage_a2(c):
            pg = c % 2
            fmv = abf(A_FM[pg], 960)
            _w(V, pe, tr_tok[c])
            tr_free[pg] = ve.inc(V.tensor_copy(fmv, psA_bf(pg, 960)))
            t_fm = tr_free[pg]
            dstp = psBa[0:120, pg * 1024:pg * 1024 + 1024]
            wd(T, t_wo)
            _w(T, ve, t_fm)
            _w(T, ve, psB_free[pg])
            if c < 2:
                _w(T, se, max(sc_free))
            for kt in range(8):
                for j in range(2):
                    mm = T.matmul(
                        dstp[:, j * 512:(j + 1) * 512],
                        fmv[:, kt * 120:(kt + 1) * 120],
                        abf(A_WO, 8192)[:, kt * 1024 + j * 512:kt * 1024 + (j + 1) * 512],
                        start=(kt == 0), stop=(kt == 7))
                    if j == 1:
                        nc.mark_noldw()
            wps_tok[c] = pe.inc(mm)

        def ph5_stage_b(c):
            pg = c % 2
            dstp = psBa[0:120, pg * 1024:pg * 1024 + 1024]
            sv = s_tm.ap()[0:120, pg * 1024:pg * 1024 + 1024]
            _w(V, pe, wps_tok[c])
            wd(V, CT)
            if c >= 2:
                wd(V, store_tok[c - 2])
            _w(V, se, gtmp_free[pg])
            t_sadd = ve.inc(V.tensor_add(sv, dstp, boutb[0:120, :]))
            psB_free[pg] = t_sadd
            if DEBUG and st == DEBUG_ST:
                _w(SY, ve, t_sadd)
                stm.inc(SY.dma_start(dbg_s[c * 120:c * 120 + 120, :], sv))
                _w(V, stm, stm.n)
            _w(V, se, t_g_ready[st])
            _w(V, ve, t_sadd)
            t_smul = ve.inc(V.tensor_mul(sv, sv, g_buf[0:120, c * 1024:(c + 1) * 1024]))
            wd(V, xr_tok[c])
            _w(V, ve, t_smul)
            t_comb = ve.inc(V.tensor_add(sv, sv, af32(A_XR[pg], 1024, 120)))
            comb_tok[pg] = t_comb
            _w(SY, ve, t_comb)
            store_tok[c] = (sts[pg], sts[pg].inc(SY.dma_start(
                out_d[base + c * 120:base + c * 120 + 120, :], sv)))
            if c + 2 < NCH:
                xr_tok[c + 2] = (xl[pg], xl[pg].inc(SY.dma_start(
                    af32(A_XR[pg], 1024, 120),
                    xin[base + (c + 2) * 120:base + (c + 2) * 120 + 120, :])))

        ph5_stage_a(0)
        ph5_stage_a(1)
        for c in range(NCH):
            ph5_stage_a2(c)
            if c + 2 < NCH:
                ph5_stage_a(c + 2)
            if c >= 1:
                ph5_stage_b(c - 1)
        ph5_stage_b(NCH - 1)

        if st == 1:
            ratm = abf(A_RATM, 1024, 40)
            rr40 = af32(A_RR, 16, 40)
            gr40 = abf(A_RG, 1024, 40)
            xr40 = af32(A_RXR, 1024, 40)
            _w(V, ve, ve.n)
            t_z = ve.inc(V.memset(gr40[0:40, :], 0.0))
            _w(SY, se, t_g_ready[0])
            _w(SY, se, t_g_ready[1])
            _w(SY, ve, t_z)
            stm.inc(SY.dma_start(gr40[0:8, :], g_r0[0:8, :]))
            t12 = (stm, stm.inc(SY.dma_start(gr40[20:28, :], g_r1[0:8, :])))
            xl[0].inc(SY.dma_start(xr40[0:8, :], xin[2040:2048, :]))
            t34 = (xl[0], xl[0].inc(SY.dma_start(xr40[20:28, :], xin[4088:4096, :])))
            _w(S, ve, ve.n)
            _w(S, se, last_rr)
            t_ln = se.inc(S.activation(stat[0:40, 0:16], rr40[0:40, 0:16], AF.Ln))
            _w(S, se, t_ln)
            t_rr = se.inc(S.activation(r_rec[0:40, 0:16], stat[0:40, 0:16],
                                       AF.Exp, scale=-1.0))
            nrm = abf(A_NRM[0], 1024, 40)
            _w(V, se, t_rr)
            for h in range(16):
                tn = V.tensor_scalar_mul(nrm[0:40, h * 64:(h + 1) * 64],
                                         ratm[0:40, h * 64:(h + 1) * 64],
                                         r_rec[0:40, h:h + 1])
            t_nrm = ve.inc(tn)
            _w(T, ve, t_nrm)
            _w(T, ve, tr_free[0])
            for i in range(8):
                mm = T.matmul(psA_bf(0, 320)[:, i * 40:(i + 1) * 40],
                              nrm[0:40, i * 128:(i + 1) * 128], identt[0:40, 0:40],
                              is_transpose=True, start=(i == 0), stop=(i == 7),
                              skip_group_check=True)
            t_tr = pe.inc(mm)
            fmv = abf(A_FM[0], 320)
            _w(V, pe, t_tr)
            tr_free[0] = ve.inc(V.tensor_copy(fmv, psA_bf(0, 320)))
            dstp = psBa[0:40, 0:1024]
            wd(T, t_wo)
            _w(T, ve, tr_free[0])
            _w(T, ve, psB_free[0])
            for kt in range(8):
                for j in range(2):
                    mm = T.matmul(
                        dstp[:, j * 512:(j + 1) * 512],
                        fmv[:, kt * 40:(kt + 1) * 40],
                        abf(A_WO, 8192)[:, kt * 1024 + j * 512:kt * 1024 + (j + 1) * 512],
                        start=(kt == 0), stop=(kt == 7))
                    if j == 1:
                        nc.mark_noldw()
            t_wps = pe.inc(mm)
            sv = s_tm.ap()[0:40, 0:1024]
            _w(V, pe, t_wps)
            wd(V, t12)
            wd(V, store_tok[NCH - 1])
            wd(V, t34)
            wd(V, CT)
            t_a1 = ve.inc(V.tensor_add(sv, dstp, boutb[0:40, :]))
            _w(V, ve, t_a1)
            t_a2 = ve.inc(V.tensor_mul(sv, sv, gr40[0:40, :]))
            _w(V, ve, t_a2)
            t_comb = ve.inc(V.tensor_add(sv, sv, xr40[0:40, :]))
            _w(SY, ve, t_comb)
            stm.inc(SY.dma_start(out_d[2040:2048, :], sv[0:8, :]))
            stm.inc(SY.dma_start(out_d[4088:4096, :], sv[20:28, :]))

    assert ring_ci[0] == len(ring_plan) == len(ring_toks)
    for _sem in (sts[0], sts[1], stm):
        _w(SY, _sem, _sem.n)
    ctx.close()
    return nc


_NC_CACHE = None


def _get_nc():
    global _NC_CACHE
    if _NC_CACHE is None:
        _NC_CACHE = build_nc()
    return _NC_CACHE


def kernel(L0_attn_x, L0_attn_Wqkv, L0_attn_Wout, L0_attn_bout,
           L0_ffn_w_gate, L0_ffn_w_up, L0_ffn_w_down):
    x = np.asarray(L0_attn_x, dtype=np.float32)
    imgs = _weight_images(L0_attn_Wqkv, L0_attn_Wout,
                          np.asarray(L0_ffn_w_gate, dtype=np.int32),
                          np.asarray(L0_ffn_w_up, dtype=np.int32),
                          np.asarray(L0_ffn_w_down, dtype=np.int32),
                          L0_attn_bout)
    order = _perm_order()
    nc = _get_nc()
    in_maps = []
    for c in range(8):
        m = {"x_perm": np.ascontiguousarray(x[c][order]),
             "wqk": imgs["wqk"], "wv": imgs["wv"], "wo": imgs["wo"],
             "wg": imgs["wg"], "wu": imgs["wu"], "wd": imgs["wd"],
             "boutb": imgs["boutb"], "ident": imgs["ident"],
             "amask": imgs["amask"]}
        in_maps.append(m)
    res = run_bass_kernel_spmd(nc, in_maps, core_ids=list(range(8)))
    inv = np.empty_like(order)
    inv[order] = np.arange(NTOK)
    out = np.empty((8, NTOK, D), dtype=np.float32)
    for c in range(8):
        out[c] = np.asarray(res.results[c]["out_perm"])[inv]
    return out
